# version 13
# baseline (speedup 1.0000x reference)
import math
import numpy as np
import ml_dtypes
import concourse.bass as bass
import concourse.mybir as mybir
from concourse.bass_utils import run_bass_kernel_spmd

F32 = mybir.dt.float32
BF16 = mybir.dt.bfloat16
I32 = mybir.dt.int32
AF = mybir.ActivationFunctionType
ALU = mybir.AluOpType

NF4 = [-1.0, -0.6961928009986877, -0.5250730514526367, -0.39491748809814453,
       -0.28444138169288635, -0.18477343022823334, -0.09105003625154495, 0.0,
       0.07958029955625534, 0.16093020141124725, 0.2461123019456863,
       0.33791524171829224, 0.44070982933044434, 0.5626170039176941,
       0.7229568362236023, 1.0]

NCORES = 8
T = 1024
E = 4096
DFF = 11008
NH = 8
SCALE = 1.0 / math.sqrt(128.0)
EXP_BIAS = -20.0
DFF_PAD = 11264
DSH = 1408
GUSH = 512
ECH = E // 128
FCH = DFF // 128
DCH = DSH // 128
EPS = 1e-5

DEBUG = False
PHASE_STOP = "full"


class Sem:
    def __init__(self, cm):
        self._cm = cm
        self.h = cm.__enter__()
        self.n = 0

    def bump(self, k):
        self.n += k
        return self.n


def build_program():
    nc = bass.Bass("TRN2", target_bir_lowering=False, num_devices=NCORES)

    def inp(name, shape, dt=F32):
        return nc.dram_tensor(name, list(shape), dt, kind="ExternalInput").ap()

    xTs = inp("xTs", (E, T))
    WQ = inp("WQ", (ECH, 128, E))
    WK = inp("WK", (NH, 128, E))
    WV = inp("WV", (NH, 128, E))
    WO = inp("WO", (ECH, 128, E))
    bo_in = inp("bo_r", (128, ECH))
    gcT = inp("gcT", (GUSH, DFF), I32)
    ucT = inp("ucT", (GUSH, DFF), I32)
    dcT = inp("dcT", (DSH, E), I32)
    gax = inp("gax", (8, DFF))
    uax = inp("uax", (8, DFF))
    dax = inp("dax", (22, E))
    Ag = inp("Ag", (128, 512))
    Au = inp("Au", (128, 512))
    Ad = inp("Ad", (128, 16 * FCH))
    Bg = inp("Bg", (16, DFF))
    Bu = inp("Bu", (16, DFF))
    Bd = inp("Bd", (16, E))
    ident_in = inp("ident", (128, 128), BF16)

    outT = nc.dram_tensor("outT", [E, T], F32, kind="ExternalOutput").ap()
    dbg = {}
    if DEBUG:
        for nm, shape, dt in (("d_q", (E, T), BF16), ("d_kv", (T, 2048), BF16),
                              ("d_ot", (E, T), BF16),
                              ("d_hpre", (E, T), F32), ("d_h1", (E, T), F32),
                              ("d_x3", (512, T), BF16), ("d_gu", (1024, DFF), BF16),
                              ("d_dd", (256, E), BF16), ("d_h2", (E, T), F32)):
            dbg[nm] = nc.dram_tensor(nm, list(shape), dt, kind="ExternalOutput").ap()

    kv_con = nc.dram_tensor("kv_con", [T, 2048], BF16).ap()
    kv_g = nc.dram_tensor("kv_g", [NCORES * T, 2048], BF16, addr_space="Shared").ap()
    gu_con = nc.dram_tensor("gu_con", [2 * GUSH, DFF], BF16).ap()
    gu_g = nc.dram_tensor("gu_g", [NCORES * 2 * GUSH, DFF], BF16, addr_space="Shared").ap()
    d_con = nc.dram_tensor("d_con", [DSH, E], BF16).ap()
    d_g = nc.dram_tensor("d_g", [DFF_PAD, E], BF16, addr_space="Shared").ap()
    q_scr = nc.dram_tensor("q_scr", [E, T], BF16).ap()
    hpre_scr = nc.dram_tensor("hpre_scr", [E, T], F32).ap()
    h1f_scr = nc.dram_tensor("h1f_scr", [E, T], F32).ap()
    h2p_scr = nc.dram_tensor("h2p_scr", [E, T], F32).ap()
    x3_scr = nc.dram_tensor("x3_scr", [DFF, T], BF16).ap()

    off = [16384]

    def sb(name, shape, dt, at=None, align=32):
        nbytes = int(np.prod(shape[1:])) * mybir.dt.size(dt)
        if at is None:
            at = (off[0] + align - 1) // align * align
            off[0] = at + nbytes
        t = nc.alloc_sbuf_tensor_at(name, list(shape), dt, offset=at)
        return t.ap()

    ones_b = sb("ones_b", [128, 1], BF16)
    cbias = sb("cbias", [128, 2], F32)
    ones_f = sb("ones_f", [128, 1], F32)
    onesr_f = sb("onesr_f", [128, 128], F32)
    ident = sb("ident_sb", [128, 128], BF16)
    bo_sb = sb("bo_sb", [128, ECH], F32)
    rowA = sb("rowA", [128, 1024], F32)
    rowB = sb("rowB", [128, 1024], F32)
    xabuf = sb("xabuf", [128, 1024], BF16)
    lorb = sb("lorb", [128, 1024], BF16)
    PT2 = sb("PT2", [128, 2048], BF16)
    x3f1 = sb("x3f1", [128, 2048], BF16)
    recipb = sb("recipb", [128, 1024], F32)
    rbuf = sb("rbuf", [128, 1024], F32)
    kvstage = sb("kvstage", [128, 2048], BF16)

    deq0 = off[0]
    codes8 = sb("codes8", [128, 3072], BF16)
    dacc = sb("dacc", [128, 1024], F32)
    dtmp = sb("dtmp", [128, 1024], F32)
    absb4 = sb("absb4", [128, 2048], F32)
    dout2 = sb("dout2", [128, 2048], BF16)
    h2t = sb("h2t", [128, 4096], F32, at=deq0)
    x3s2 = sb("x3s2", [128, 2048], BF16, at=deq0 + 16384)
    wdt = sb("wdt", [128, 1024], BF16, at=deq0 + 16384 + 4096)

    mid0 = off[0]
    stA = sb("stA", [128, 2048], F32)
    stB = sb("stB", [128, 2048], F32)
    VhB = sb("VhB", [128, 8192], BF16, at=mid0)
    stC = sb("stC", [128, 1024], F32)
    wblk2 = sb("wblk2", [128, 8192], BF16)
    r10 = off[0]
    xT_b = sb("xT_b", [128, 32768], BF16)
    OT_b = sb("OT_b", [128, 32768], BF16, at=r10)
    h1b = sb("h1b", [128, 32768], BF16, at=r10)
    r20 = r10 + 65536
    Kh2 = sb("Kh2", [128, 16384], BF16, at=r20)
    wgu = sb("wgu", [128, 16384], BF16, at=r20)
    Vh1 = sb("Vh1", [128, 8192], BF16, at=r20 + 32768)
    total_sbuf = r20 + 32768 + 16384
    assert total_sbuf <= 229376, f"sbuf over budget: {total_sbuf}"

    PS = [nc.alloc_psum_tensor(f"PS{i}", [128, 1024], F32).ap() for i in range(4)]
    PSbt = [p.bitcast(BF16) for p in PS]

    sL = Sem(nc.semaphore("sL"))
    sG = Sem(nc.semaphore("sG"))
    sC = Sem(nc.semaphore("sC"))
    sMM = Sem(nc.semaphore("sMM"))
    sV = Sem(nc.semaphore("sV"))
    sA = Sem(nc.semaphore("sA"))

    plan = {"tensor": [], "vector": [], "scalar": [], "gpsimd": [], "sync": []}

    def on(eng, fn):
        plan[eng].append(fn)

    def dma(q, sem, out, in_, **kw):
        tgt = sem.bump(16)
        on(q, lambda e: e.dma_start(out=out, in_=in_, **kw).then_inc(sem.h, 16))
        return tgt

    def wait(eng, sem, val):
        if val > 0:
            on(eng, lambda e: e.wait_ge(sem.h, val))

    def mm(o, l, r, st, sp):
        on("tensor", lambda e: e.matmul(o, l, r, start=st, stop=sp))

    def inc(eng, sem):
        tgt = sem.bump(1)
        if plan[eng]:
            prev = plan[eng][-1]
            plan[eng][-1] = (lambda e, p=prev: p(e).then_inc(sem.h, 1))
        else:
            on(eng, lambda e: e.nop().then_inc(sem.h, 1))
        return tgt

    deq_state = {"g": 0, "dv": {}, "st": {}}
    deq_groups = []
    for csrc, axs, cdst, n_ch in ((gcT, gax, 0, 4), (ucT, uax, GUSH, 4)):
        for ch in range(n_ch):
            for fi in range(11):
                f0 = fi * 1024
                w = min(1024, DFF - f0)
                deq_groups.append(("gu", csrc[128 * ch:128 * (ch + 1), f0:f0 + w],
                                   axs, 2 * ch, f0, w,
                                   gu_con[cdst + 128 * ch:cdst + 128 * (ch + 1), f0:f0 + w]))
    for ch in range(DCH):
        for fi in range(4):
            f0 = fi * 1024
            deq_groups.append(("d", dcT[128 * ch:128 * (ch + 1), f0:f0 + 1024],
                               dax, 2 * ch, f0, 1024,
                               d_con[128 * ch:128 * (ch + 1), f0:f0 + 1024]))
    N_GU_GROUPS = 88

    deq_enabled = PHASE_STOP in ("pq", "pk", "p1", "f1", "full")
    if PHASE_STOP in ("pq", "pk", "p1"):
        deq_enabled = PHASE_STOP == "p1"

    def emit_deq(n):
        if not deq_enabled:
            return
        s = deq_state
        for _ in range(n):
            g = s["g"]
            if g >= len(deq_groups):
                return
            kind, csrc, axs, axr, f0, w, cdst = deq_groups[g]
            cb = codes8[:, 1024 * (g % 3):1024 * (g % 3) + w]
            ab = absb4[:, 1024 * (g % 2):1024 * (g % 2) + w]
            ob = dout2[:, 1024 * (g % 2):1024 * (g % 2) + w]
            wait("gpsimd", sV, s["dv"].get(g - 3, 0))
            cl = dma("gpsimd", sG, cb, csrc)
            wait("gpsimd", sV, s["dv"].get(g - 2, 0))
            a1 = dma("gpsimd", sG, ab[0:64, :], axs[axr:axr + 1, f0:f0 + w].partition_broadcast(64))
            a2 = dma("gpsimd", sG, ab[64:128, :], axs[axr + 1:axr + 2, f0:f0 + w].partition_broadcast(64))
            wait("vector", sG, cl)
            wait("vector", sL, s["st"].get(g - 2, 0))
            on("vector", lambda v, cb=cb: v.tensor_scalar(
                dacc[:, 0:cb.shape[1]], cb, 0.0, float(NF4[0]), ALU.is_equal, ALU.mult))
            for j in range(1, 16):
                on("vector", lambda v, cb=cb, j=j: v.tensor_scalar(
                    dtmp[:, 0:cb.shape[1]], cb, float(j), float(NF4[j]), ALU.is_equal, ALU.mult))
                on("vector", lambda v, w=w: v.tensor_tensor(
                    dacc[:, 0:w], dacc[:, 0:w], dtmp[:, 0:w], ALU.add))
            wait("vector", sG, a2)
            on("vector", lambda v, ob=ob, ab=ab, w=w: v.tensor_tensor(
                ob, dacc[:, 0:w], ab, ALU.mult))
            s["dv"][g] = inc("vector", sV)
            wait("sync", sV, s["dv"][g])
            s["st"][g] = dma("sync", sL, cdst, ob)
            s["g"] = g + 1
            if s["g"] == N_GU_GROUPS:
                wait("gpsimd", sL, s["st"][N_GU_GROUPS - 1])
                on("gpsimd", lambda gg: gg.collective_compute(
                    "AllGather", ALU.bypass, replica_groups=[list(range(NCORES))],
                    ins=[gu_con], outs=[gu_g]).then_inc(sC.h, 1))
                sC.bump(1)
            if s["g"] == len(deq_groups):
                wait("gpsimd", sL, s["st"][len(deq_groups) - 1])
                on("gpsimd", lambda gg: gg.collective_compute(
                    "AllGather", ALU.bypass, replica_groups=[list(range(NCORES))],
                    ins=[d_con], outs=[d_g]).then_inc(sC.h, 1))
                sC.bump(1)

    on("vector", lambda v: v.memset(ones_b, 1.0))
    on("vector", lambda v: v.memset(ones_f, 1.0))
    on("vector", lambda v: v.memset(onesr_f, 1.0))
    on("vector", lambda v: v.memset(cbias[:, 0:1], EXP_BIAS))
    on("vector", lambda v: v.memset(cbias[:, 1:2], EPS))
    consts_v = inc("vector", sV)
    ident_ld = dma("sync", sL, ident, ident_in)
    bo_ld = dma("sync", sL, bo_sb, bo_in)

    for c in range(ECH):
        dma("gpsimd", sG, xT_b[:, 1024 * c:1024 * (c + 1)], xTs[128 * c:128 * (c + 1), :])
    xT_done = sG.n

    p1_mm_done = {}
    p1_sink_done = {}
    p1_gd = [0]

    def proj_stream(win, n_dch, sink, deq_every=0):
        mm_done = p1_mm_done
        sink_done = p1_sink_done
        for d in range(n_dch):
            gd = p1_gd[0]
            p1_gd[0] += 1
            wb = wblk2[:, 4096 * (gd % 2):4096 * (gd % 2 + 1)]
            wait("gpsimd", sMM, mm_done.get(gd - 2, 0))
            wld = dma("gpsimd", sG, wb, win[d])
            wait("tensor", sG, wld)
            if gd == 0:
                wait("tensor", sG, xT_done)
            ps = PS[gd % 2]
            tok = sink_done.get(gd - 2)
            if tok is not None:
                wait("tensor", tok[0], tok[1])
            for c in range(ECH):
                for half in range(2):
                    mm(ps[:, 512 * half:512 * (half + 1)],
                       wb[:, 128 * c:128 * (c + 1)],
                       xT_b[:, 1024 * c + 512 * half:1024 * c + 512 * (half + 1)],
                       c == 0, c == ECH - 1)
            mm_done[gd] = inc("tensor", sMM)
            wait("vector", sMM, mm_done[gd])
            tok = sink_done.get(gd - 2)
            if tok is not None:
                wait("vector", tok[0], tok[1])
            stg = kvstage[:, 1024 * (gd % 2):1024 * (gd % 2 + 1)]
            on("vector", lambda v, s=stg, p=ps: v.tensor_copy(s, p))
            ev = inc("vector", sV)
            sink_done[gd] = sink(d, stg, ev)
            if deq_every and d % deq_every == deq_every - 1:
                emit_deq(1)

    def q_sink(d, stg, ev):
        wait("sync", sV, ev)
        t = dma("sync", sL, q_scr[128 * d:128 * (d + 1), :], stg)
        return (sL, t)

    def k_sink(d, stg, ev):
        wait("sync", sV, ev)
        t = dma("sync", sL, kv_con[128 * d:128 * (d + 1), 0:1024], stg)
        return (sL, t)

    tr_prev = {}

    tr_store = {}

    def v_sink(d, stg, ev):
        wait("tensor", sV, ev)
        if d == 0:
            wait("tensor", sL, ident_ld)
        last = None
        for sub in range(8):
            gsub = 8 * d + sub
            pb = PSbt[2][:, 1024 * (sub % 2):1024 * (sub % 2) + 128]
            tok = tr_prev.get(gsub - 2)
            if tok:
                wait("tensor", sV, tok)
            on("tensor", lambda e, o=pb, i=stg[:, 128 * sub:128 * (sub + 1)]:
               e.transpose(o, i, ident))
            td = inc("tensor", sMM)
            wait("vector", sMM, td)
            wait("vector", sL, tr_store.get(gsub - 2, 0))
            tout = x3f1[:, 128 * (sub % 2):128 * (sub % 2) + 128]
            on("vector", lambda v, o=tout, p=pb: v.tensor_copy(o, p))
            ev2 = inc("vector", sV)
            tr_prev[gsub] = ev2
            wait("sync", sV, ev2)
            last = dma("sync", sL,
                       kv_con[128 * sub:128 * (sub + 1), 1024 + 128 * d:1024 + 128 * (d + 1)],
                       tout)
            tr_store[gsub] = last
        return (sL, last)

    if PHASE_STOP != "p0":
        if PHASE_STOP != "pk":
            proj_stream([WQ[d] for d in range(ECH)], ECH, q_sink, deq_every=2)
        if PHASE_STOP != "pq":
            proj_stream([WK[d] for d in range(NH)], NH, k_sink, deq_every=2)
        if PHASE_STOP not in ("pq", "pk"):
            proj_stream([WV[d] for d in range(NH)], NH, v_sink, deq_every=2)
    kv_stores_done = sL.n
    phase1_mm = sMM.n

    def dummy_out_and_emit():
        on("vector", lambda v: v.memset(stB[:, 0:1024], 0.0))
        zv = inc("vector", sV)
        wait("sync", sV, zv)
        for oc in range(ECH):
            dma("sync", sL, outT[128 * oc:128 * (oc + 1), :], stB[:, 0:1024])
        wait("sync", sL, sL.n)
        return emit_all()

    def emit_all():
        with nc.Block() as block:
            @block.tensor
            def _(e):
                for fn in plan["tensor"]:
                    fn(e)

            @block.vector
            def _(e):
                for fn in plan["vector"]:
                    fn(e)

            @block.scalar
            def _(e):
                for fn in plan["scalar"]:
                    fn(e)

            @block.gpsimd
            def _(e):
                for fn in plan["gpsimd"]:
                    fn(e)

            @block.sync
            def _(e):
                for fn in plan["sync"]:
                    fn(e)
        return nc

    if PHASE_STOP in ("p0", "pq", "pk", "p1"):
        return dummy_out_and_emit()

    wait("gpsimd", sL, kv_stores_done)
    on("gpsimd", lambda g: g.collective_compute(
        "AllGather", ALU.bypass, replica_groups=[list(range(NCORES))],
        ins=[kv_con], outs=[kv_g]).then_inc(sC.h, 1))
    kv_gathered = sC.bump(1)

    wait("tensor", sV, consts_v)
    NU = 4096
    sc_done = {}
    ep_done = {}
    avs_done = {}
    rec_ev = {}
    ot_ev = {}
    kvload = {}
    qtload = {}

    def khb(hd):
        return Kh2[:, 8192 * (hd % 2):8192 * (hd % 2) + 8192]

    def vhb(hd):
        return Vh1 if hd % 2 == 0 else VhB

    vload = {}

    def load_head(hd):
        t = 0
        for r in range(NCORES):
            t = dma("sync", sL, khb(hd)[:, 1024 * r:1024 * (r + 1)],
                    kv_g[1024 * r + 128 * hd:1024 * r + 128 * hd + 128, 0:1024])
        kvload[hd] = t

    def load_vh(hd):
        t = 0
        for r in range(NCORES):
            for sub in range(8):
                kc = 8 * r + sub
                t = dma("sync", sL, vhb(hd)[:, 128 * kc:128 * (kc + 1)],
                        kv_g[1024 * r + 128 * sub:1024 * r + 128 * sub + 128,
                             1024 + 128 * hd:1024 + 128 * hd + 128])
        vload[hd] = t

    def load_qt(qh):
        qtload[qh] = dma("sync", sL, kvstage[:, 1024 * (qh % 2):1024 * (qh % 2 + 1)],
                         q_scr[128 * qh:128 * (qh + 1), :])

    def score_bank(u):
        j = u % 4
        return PS[j // 2][:, 512 * (j % 2):512 * (j % 2) + 512]

    def pt_slice(u):
        return PT2[:, 512 * (u % 4):512 * (u % 4) + 512]

    wait("sync", sC, kv_gathered)
    wait("sync", sMM, phase1_mm)
    load_head(0)
    load_head(1)
    load_vh(0)
    load_vh(1)
    load_qt(0)
    load_qt(1)

    def emit_avs(u):
        p = u // 64
        kc = u % 64
        hd = p // 8
        wait("tensor", sA, ep_done[u // 2])
        if kc == 0 and p > 0:
            wait("tensor", sV, ot_ev[p - 1])
        if kc == 0 and p % 8 == 0:
            wait("tensor", sL, vload[hd])
        mm(PS[2][:, 0:512], vhb(hd)[:, 128 * kc:128 * (kc + 1)], pt_slice(u),
           kc == 0, kc == 63)
        mm(PS[3][0:1, 0:512], ones_b, pt_slice(u), kc == 0, kc == 63)
        avs_done[u] = inc("tensor", sMM)

    def emit_recip(p):
        wait("vector", sMM, avs_done[64 * p + 63])
        on("vector", lambda v: v.reciprocal(rowA[0:1, 0:512], PS[3][0:1, 0:512]))
        rec_ev[p] = inc("vector", sV)

    def emit_epilogue(p):
        qh = p // 2
        half = p % 2
        wait("tensor", sV, rec_ev[p])
        mm(PS[2][:, 512:1024], onesr_f[0:1, :], rowA[0:1, 0:512], True, True)
        bc = inc("tensor", sMM)
        wait("vector", sMM, bc)
        on("vector", lambda v: v.tensor_copy(recipb[:, 0:512], PS[2][:, 512:1024]))
        on("vector", lambda v, qh=qh, half=half: v.tensor_tensor(
            OT_b[:, 1024 * qh + 512 * half:1024 * qh + 512 * half + 512],
            PS[2][:, 0:512], recipb[:, 0:512], ALU.mult))
        ot_ev[p] = inc("vector", sV)
        emit_deq(2 if deq_state["g"] < N_GU_GROUPS else 1)

    for u in range(NU):
        p = u // 64
        qh = p // 2
        half = p % 2
        kc = u % 64
        hd = qh // 4
        if u % 512 == 0:
            wait("tensor", sL, kvload[hd])
        if u % 128 == 0:
            wait("tensor", sL, qtload[qh])
        if u >= 4:
            wait("tensor", sA, ep_done[(u - 4) // 2])
        mm(score_bank(u), khb(hd)[:, 128 * kc:128 * (kc + 1)],
           kvstage[:, 1024 * (qh % 2) + 512 * half:1024 * (qh % 2) + 512 * half + 512],
           True, True)
        sc_done[u] = inc("tensor", sMM)
        if u % 512 == 3 and hd >= 1 and hd + 1 < NH:
            wait("sync", sMM, sc_done[512 * hd - 1])
            load_head(hd + 1)
        if u % 128 == 3 and qh >= 1 and qh + 1 < 32:
            wait("sync", sMM, sc_done[128 * qh - 1])
            load_qt(qh + 1)
        if u % 2 == 1:
            q2 = u // 2
            wait("scalar", sMM, sc_done[u])
            if q2 >= 2:
                wait("scalar", sMM, avs_done.get(2 * q2 - 3, 0))
            pj = (u - 1) % 4
            psrc = PS[pj // 2]
            pdst = PT2[:, 512 * pj:512 * pj + 1024]
            on("scalar", lambda s, o=pdst, i=psrc: s.activation(
                o, i, AF.Exp, bias=cbias[:, 0:1], scale=SCALE))
            ep_done[q2] = inc("scalar", sA)
        if u >= 3:
            u2 = u - 3
            emit_avs(u2)
            if u2 % 64 == 63:
                emit_recip(u2 // 64)
                emit_epilogue(u2 // 64)
            if u2 % 512 == 2 and u2 >= 512 and u2 // 512 + 1 < NH:
                wait("sync", sMM, avs_done[512 * (u2 // 512) - 1])
                load_vh(u2 // 512 + 1)
    for u2 in range(NU - 3, NU):
        emit_avs(u2)
        if u2 % 64 == 63:
            emit_recip(u2 // 64)
            emit_epilogue(u2 // 64)
    flash_last_mm = sMM.n
    flash_done_v = ot_ev[63]
    if PHASE_STOP == "flash":
        return dummy_out_and_emit()

    if DEBUG:
        wait("sync", sV, flash_done_v)
        for oc in range(ECH):
            dma("sync", sL, dbg["d_ot"][128 * oc:128 * (oc + 1), :],
                OT_b[:, 1024 * oc:1024 * (oc + 1)])
    wait("gpsimd", sV, flash_done_v)
    copy_done = {}
    hp_evs = {}
    hp_store = {}
    ln1_mm = {}
    for oc in range(ECH):
        wait("gpsimd", sMM, ln1_mm.get(oc - 2, 0))
        wld = dma("gpsimd", sG, wblk2[:, 4096 * (oc % 2):4096 * (oc % 2 + 1)], WO[oc])
        wait("sync", sV, hp_evs.get(oc - 2, 0))
        if oc == 0:
            wait("sync", sMM, flash_last_mm)
        xld = dma("sync", sL, stB[:, 1024 * (oc % 2):1024 * (oc % 2 + 1)],
                  xTs[128 * oc:128 * (oc + 1), :])
        wait("tensor", sG, wld)
        if oc == 0:
            wait("tensor", sV, flash_done_v)
            wait("tensor", sL, bo_ld)
        ps = PS[oc % 2]
        wait("tensor", sV, hp_evs.get(oc - 2, 0))
        for c in range(ECH):
            for half in range(2):
                mm(ps[:, 512 * half:512 * (half + 1)],
                   wblk2[:, 4096 * (oc % 2) + 128 * c:4096 * (oc % 2) + 128 * (c + 1)],
                   OT_b[:, 1024 * c + 512 * half:1024 * c + 512 * (half + 1)],
                   c == 0, c == ECH - 1)
        mmd = inc("tensor", sMM)
        wait("vector", sMM, mmd)
        wait("vector", sL, xld)
        wait("vector", sL, hp_store.get(oc - 2, 0))
        wait("vector", sMM, ln1_mm.get(oc - 2, 0))
        hp = stA[:, 1024 * (oc % 2):1024 * (oc % 2 + 1)]
        on("vector", lambda v, hp=hp, p=ps, oc=oc: v.scalar_tensor_tensor(
            hp, p, bo_sb[:, oc:oc + 1], stB[:, 1024 * (oc % 2):1024 * (oc % 2 + 1)],
            ALU.add, ALU.add))
        hp_evs[oc] = inc("vector", sV)
        wait("sync", sV, hp_evs[oc])
        hp_store[oc] = dma("sync", sL, hpre_scr[128 * oc:128 * (oc + 1), :], hp)
        wait("scalar", sV, hp_evs[oc])
        wait("scalar", sMM, ln1_mm.get(oc - 1, 0))
        on("scalar", lambda s, hp=hp: s.activation(stC, hp, AF.Square, bias=0.0, scale=1.0))
        sq_ev = inc("scalar", sA)
        wait("tensor", sA, sq_ev)
        for half in range(2):
            mm(PS[2][0:1, 512 * half:512 * (half + 1)], ones_f,
               hp[:, 512 * half:512 * (half + 1)], oc == 0, oc == ECH - 1)
        for half in range(2):
            mm(PS[3][0:1, 512 * half:512 * (half + 1)], ones_f,
               stC[:, 512 * half:512 * (half + 1)], oc == 0, oc == ECH - 1)
        ln1_mm[oc] = inc("tensor", sMM)
        emit_deq(1)
    ln1_mm_final = ln1_mm[ECH - 1]

    def ln_rows(psum_z_row, psum_z2_row, gate_mm):
        wait("vector", sMM, gate_mm)
        on("vector", lambda v: v.tensor_scalar(rowA[32:33, :], psum_z_row,
                                               1.0 / E, None, ALU.mult))
        on("vector", lambda v: v.tensor_scalar(rowB[0:1, :], psum_z2_row,
                                               1.0 / E, None, ALU.mult))
        on("vector", lambda v: v.tensor_tensor(rowA[0:1, :], rowA[32:33, :],
                                               rowA[32:33, :], ALU.mult))
        on("vector", lambda v: v.tensor_tensor(rowB[0:1, :], rowB[0:1, :],
                                               rowA[0:1, :], ALU.subtract))
        var_ev = inc("vector", sV)
        wait("scalar", sV, var_ev)
        on("scalar", lambda s: s.activation(rowB[0:1, :], rowB[0:1, :],
                                            AF.Sqrt, bias=cbias[0:1, 1:2], scale=1.0))
        sq2 = inc("scalar", sA)
        wait("vector", sA, sq2)
        on("vector", lambda v: v.reciprocal(rowA[64:65, :], rowB[0:1, :]))
        rr_ev = inc("vector", sV)
        wait("tensor", sV, rr_ev)
        for half in range(2):
            mm(PS[0][:, 512 * half:512 * (half + 1)], onesr_f[32:33, :],
               rowA[32:33, 512 * half:512 * (half + 1)], True, True)
            mm(PS[1][:, 512 * half:512 * (half + 1)], onesr_f[64:65, :],
               rowA[64:65, 512 * half:512 * (half + 1)], True, True)
        bc2 = inc("tensor", sMM)
        wait("vector", sMM, bc2)
        on("vector", lambda v: v.tensor_copy(recipb, PS[0]))
        on("vector", lambda v: v.tensor_copy(rbuf, PS[1]))
        return inc("vector", sV)

    mb_ev = ln_rows(PS[2][0:1, :], PS[3][0:1, :], ln1_mm_final)

    h1_store = {}
    h1_evs = {}
    for oc in range(ECH):
        wait("sync", sV, h1_evs.get(oc - 2, 0))
        if oc == 0:
            wait("sync", sMM, ln1_mm_final)
        hl = dma("sync", sL, stA[:, 1024 * (oc % 2):1024 * (oc % 2 + 1)],
                 hpre_scr[128 * oc:128 * (oc + 1), :])
        wait("vector", sL, hl)
        wait("vector", sL, h1_store.get(oc - 2, 0))
        if oc == 0:
            wait("vector", sV, mb_ev)
        hp = stA[:, 1024 * (oc % 2):1024 * (oc % 2 + 1)]
        t1 = stB[:, 1024 * (oc % 2):1024 * (oc % 2 + 1)]
        on("vector", lambda v, hp=hp, t1=t1: v.tensor_tensor(t1, hp, recipb, ALU.subtract))
        on("vector", lambda v, t1=t1: v.tensor_tensor(t1, t1, rbuf, ALU.mult))
        on("vector", lambda v, t1=t1, oc=oc: v.tensor_copy(
            h1b[:, 1024 * oc:1024 * (oc + 1)], t1))
        h1_evs[oc] = inc("vector", sV)
        wait("sync", sV, h1_evs[oc])
        h1_store[oc] = dma("sync", sL, h1f_scr[128 * oc:128 * (oc + 1), :], t1)
        if oc % 2 == 1:
            emit_deq(1)
    h1_ready = h1_evs[ECH - 1]
    if PHASE_STOP == "ln1":
        return dummy_out_and_emit()

    wait("gpsimd", sMM, ln1_mm_final)
    agl = dma("gpsimd", sG, wgu[:, 0:512], Ag)
    aul = dma("gpsimd", sG, wgu[:, 512:1024], Au)
    wait("tensor", sG, aul)
    wait("tensor", sV, h1_ready)
    for c in range(ECH):
        for half in range(2):
            mm(PS[2][0:16, 512 * half:512 * (half + 1)],
               wgu[:, 16 * c:16 * (c + 1)],
               h1b[:, 1024 * c + 512 * half:1024 * c + 512 * (half + 1)],
               c == 0, c == ECH - 1)
        for half in range(2):
            mm(PS[3][0:16, 512 * half:512 * (half + 1)],
               wgu[:, 512 + 16 * c:512 + 16 * (c + 1)],
               h1b[:, 1024 * c + 512 * half:1024 * c + 512 * (half + 1)],
               c == 0, c == ECH - 1)
    xa_mm = inc("tensor", sMM)
    wait("vector", sMM, xa_mm)
    on("vector", lambda v: v.tensor_copy(xabuf[0:16, :], PS[2][0:16, :]))
    on("vector", lambda v: v.tensor_copy(xabuf[32:48, :], PS[3][0:16, :]))
    xa_ev = inc("vector", sV)
    emit_deq(len(deq_groups))
    gu_gathered = 2
    d_gathered = 3
    assert sC.n == 3

    f1mm = {}
    f1_x3 = {}
    x3_store = {}
    relu_done = {}
    for fc in range(FCH):
        bb = 8192 * (fc % 2)
        wait("sync", sMM, f1mm.get(fc - 2, 0))
        if fc == 0:
            wait("sync", sC, gu_gathered)
            wait("sync", sMM, xa_mm)
        wl = 0
        wlg = 0
        for r in range(NCORES):
            wl = dma("sync", sL,
                     wgu[:, bb + 512 * r: bb + 512 * (r + 1)].rearrange("p (c f) -> p c f", f=128),
                     gu_g[1024 * r:1024 * r + 512,
                          128 * fc:128 * (fc + 1)].rearrange("(c p) f -> p c f", p=128))
        for r in range(NCORES):
            wl = dma("sync", sL,
                     wgu[:, bb + 4096 + 512 * r: bb + 4096 + 512 * (r + 1)].rearrange("p (c f) -> p c f", f=128),
                     gu_g[1024 * r + 512:1024 * r + 1024,
                          128 * fc:128 * (fc + 1)].rearrange("(c p) f -> p c f", p=128))
        wait("gpsimd", sMM, f1mm.get(fc - 2, 0))
        bgl = dma("gpsimd", sG, lorb[0:16, 128 * (fc % 2):128 * (fc % 2) + 128],
                  Bg[:, 128 * fc:128 * (fc + 1)])
        bul = dma("gpsimd", sG, lorb[32:48, 128 * (fc % 2):128 * (fc % 2) + 128],
                  Bu[:, 128 * fc:128 * (fc + 1)])
        wait("tensor", sL, wl)
        wait("tensor", sG, bul)
        if fc == 0:
            wait("tensor", sV, xa_ev)
        psg = PS[0] if fc % 2 == 0 else PS[2]
        psu = PS[1] if fc % 2 == 0 else PS[3]
        wait("tensor", sV, f1_x3.get(fc - 2, 0))
        for c in range(ECH):
            for half in range(2):
                mm(psg[:, 512 * half:512 * (half + 1)],
                   wgu[:, bb + 128 * c:bb + 128 * (c + 1)],
                   h1b[:, 1024 * c + 512 * half:1024 * c + 512 * (half + 1)],
                   c == 0, False)
            for half in range(2):
                mm(psu[:, 512 * half:512 * (half + 1)],
                   wgu[:, bb + 4096 + 128 * c:bb + 4096 + 128 * (c + 1)],
                   h1b[:, 1024 * c + 512 * half:1024 * c + 512 * (half + 1)],
                   c == 0, False)
        for half in range(2):
            mm(psg[:, 512 * half:512 * (half + 1)],
               lorb[0:16, 128 * (fc % 2):128 * (fc % 2) + 128],
               xabuf[0:16, 512 * half:512 * (half + 1)], False, True)
            mm(psu[:, 512 * half:512 * (half + 1)],
               lorb[32:48, 128 * (fc % 2):128 * (fc % 2) + 128],
               xabuf[32:48, 512 * half:512 * (half + 1)], False, True)
        f1mm[fc] = inc("tensor", sMM)
        wait("scalar", sMM, f1mm[fc])
        wait("scalar", sV, f1_x3.get(fc - 2, 0))
        rt = PT2[:, 1024 * (fc % 2):1024 * (fc % 2 + 1)]
        on("scalar", lambda s, o=rt, p=psg: s.activation(o, p, AF.Relu, bias=0.0, scale=1.0))
        relu_done[fc] = inc("scalar", sA)
        wait("vector", sA, relu_done[fc])
        wait("vector", sL, x3_store.get(fc - 2, 0))
        x3t = x3f1[:, 1024 * (fc % 2):1024 * (fc % 2 + 1)]
        on("vector", lambda v, o=x3t, a=rt, p=psu: v.tensor_tensor(o, a, p, ALU.mult))
        f1_x3[fc] = inc("vector", sV)
        wait("sync", sV, f1_x3[fc])
        x3_store[fc] = dma("sync", sL, x3_scr[128 * fc:128 * (fc + 1), :], x3t)

    if PHASE_STOP == "f1":
        wait("sync", sL, x3_store[FCH - 1])
        return dummy_out_and_emit()

    wait("gpsimd", sMM, f1mm[FCH - 1])
    adl = dma("gpsimd", sG, wblk2[:, 0:16 * FCH], Ad)
    xadp = {}
    for fc in range(FCH):
        wait("sync", sMM, xadp.get(fc - 2, 0))
        xl = dma("sync", sL, x3f1[:, 1024 * (fc % 2):1024 * (fc % 2 + 1)],
                 x3_scr[128 * fc:128 * (fc + 1), :])
        wait("tensor", sL, xl)
        if fc == 0:
            wait("tensor", sG, adl)
            wait("tensor", sV, f1_x3[FCH - 1])
        for half in range(2):
            mm(PS[0][0:16, 512 * half:512 * (half + 1)],
               wblk2[:, 16 * fc:16 * (fc + 1)],
               x3f1[:, 1024 * (fc % 2) + 512 * half:1024 * (fc % 2) + 512 * (half + 1)],
               fc == 0, fc == FCH - 1)
        xadp[fc] = inc("tensor", sMM)
    wait("vector", sMM, xadp[FCH - 1])
    on("vector", lambda v: v.tensor_copy(xabuf[64:80, :], PS[0][0:16, :]))
    xad_ev = inc("vector", sV)

    stats_ev = 0
    f2p = {}
    h2_store = {}
    sz_init = False
    f2mm_prev = 0
    for s in range(8):
        wait("gpsimd", sMM, f2mm_prev)
        bdl = dma("gpsimd", sG, lorb[64:80, 512:1024], Bd[:, 512 * s:512 * (s + 1)])
        for fc in range(FCH):
            wait("sync", sMM, f2p.get((s, fc - 2), 0) if fc >= 2 else
                 f2p.get((s - 1, FCH - 1), 0))
            if s == 0 and fc == 0:
                wait("sync", sC, d_gathered)
            xl = dma("sync", sL, x3s2[:, 1024 * (fc % 2):1024 * (fc % 2 + 1)],
                     x3_scr[128 * fc:128 * (fc + 1), :])
            r, lr = fc // 11, fc % 11
            wl = dma("sync", sL, wdt[:, 512 * (fc % 2):512 * (fc % 2 + 1)],
                     d_g[DSH * r + 128 * lr:DSH * r + 128 * lr + 128, 512 * s:512 * (s + 1)])
            wait("tensor", sL, wl)
            if fc == 0:
                wait("tensor", sG, bdl)
                if s == 0:
                    wait("tensor", sV, xad_ev)
                else:
                    wait("tensor", sV, stats_ev)
            x3r = x3s2[:, 1024 * (fc % 2):1024 * (fc % 2 + 1)]
            for k in range(4):
                for half in range(2):
                    mm(PS[k][:, 512 * half:512 * (half + 1)],
                       wdt[:, 512 * (fc % 2) + 128 * k:512 * (fc % 2) + 128 * (k + 1)],
                       x3r[:, 512 * half:512 * (half + 1)], fc == 0, False)
            f2p[(s, fc)] = inc("tensor", sMM)
        for k in range(4):
            for half in range(2):
                mm(PS[k][:, 512 * half:512 * (half + 1)],
                   lorb[64:80, 512 + 128 * k:512 + 128 * (k + 1)],
                   xabuf[64:80, 512 * half:512 * (half + 1)], False, True)
        f2mm = inc("tensor", sMM)
        f2mm_prev = f2mm
        wait("vector", sMM, f2mm)
        sq_evs = {}
        for k in range(4):
            oc = 4 * s + k
            hl = dma("sync", sL, stB[:, 1024 * (k % 2):1024 * (k % 2 + 1)],
                     h1f_scr[128 * oc:128 * (oc + 1), :])
            wait("vector", sL, hl)
            wait("vector", sL, h2_store.get((s - 1, k), 0))
            on("vector", lambda v, k=k: v.tensor_tensor(
                h2t[:, 1024 * k:1024 * (k + 1)], PS[k],
                stB[:, 1024 * (k % 2):1024 * (k % 2 + 1)], ALU.add))
            ev = inc("vector", sV)
            wait("sync", sV, ev)
            h2_store[(s, k)] = dma("sync", sL, h2p_scr[128 * oc:128 * (oc + 1), :],
                                   h2t[:, 1024 * k:1024 * (k + 1)])
            sq_evs[k] = ev
        wait("tensor", sV, sq_evs[3])
        for k in range(4):
            wait("scalar", sV, sq_evs[k])
            on("scalar", lambda sc, k=k: sc.activation(
                stC, h2t[:, 1024 * k:1024 * (k + 1)], AF.Square, bias=0.0, scale=1.0))
            sqe = inc("scalar", sA)
            wait("tensor", sA, sqe)
            wait("tensor", sV, sq_evs[k])
            for half in range(2):
                mm(PS[0][0:1, 512 * half:512 * (half + 1)], ones_f,
                   h2t[:, 1024 * k + 512 * half:1024 * k + 512 * (half + 1)],
                   k == 0, k == 3)
            for half in range(2):
                mm(PS[1][0:1, 512 * half:512 * (half + 1)], ones_f,
                   stC[:, 512 * half:512 * (half + 1)], k == 0, k == 3)
            stk = inc("tensor", sMM)
            wait("scalar", sMM, stk)
        st_mm = stk
        wait("vector", sMM, st_mm)
        if not sz_init:
            on("vector", lambda v: v.tensor_copy(rowB[32:33, :], PS[0][0:1, :]))
            on("vector", lambda v: v.tensor_copy(rowB[64:65, :], PS[1][0:1, :]))
            sz_init = True
        else:
            on("vector", lambda v: v.tensor_tensor(rowB[32:33, :], rowB[32:33, :],
                                                   PS[0][0:1, :], ALU.add))
            on("vector", lambda v: v.tensor_tensor(rowB[64:65, :], rowB[64:65, :],
                                                   PS[1][0:1, :], ALU.add))
        stats_ev = inc("vector", sV)

    wait("vector", sV, stats_ev)
    on("vector", lambda v: v.tensor_scalar(rowA[32:33, :], rowB[32:33, :],
                                           1.0 / E, None, ALU.mult))
    on("vector", lambda v: v.tensor_scalar(rowB[0:1, :], rowB[64:65, :],
                                           1.0 / E, None, ALU.mult))
    on("vector", lambda v: v.tensor_tensor(rowA[0:1, :], rowA[32:33, :],
                                           rowA[32:33, :], ALU.mult))
    on("vector", lambda v: v.tensor_tensor(rowB[0:1, :], rowB[0:1, :],
                                           rowA[0:1, :], ALU.subtract))
    var2 = inc("vector", sV)
    wait("scalar", sV, var2)
    on("scalar", lambda s: s.activation(rowB[0:1, :], rowB[0:1, :],
                                        AF.Sqrt, bias=cbias[0:1, 1:2], scale=1.0))
    sq3 = inc("scalar", sA)
    wait("vector", sA, sq3)
    on("vector", lambda v: v.reciprocal(rowA[64:65, :], rowB[0:1, :]))
    rr2 = inc("vector", sV)
    wait("tensor", sV, rr2)
    wait("tensor", sV, stats_ev)
    for half in range(2):
        mm(PS[2][:, 512 * half:512 * (half + 1)], onesr_f[32:33, :],
           rowA[32:33, 512 * half:512 * (half + 1)], True, True)
        mm(PS[3][:, 512 * half:512 * (half + 1)], onesr_f[64:65, :],
           rowA[64:65, 512 * half:512 * (half + 1)], True, True)
    bc3 = inc("tensor", sMM)
    wait("vector", sMM, bc3)
    on("vector", lambda v: v.tensor_copy(recipb, PS[2]))
    on("vector", lambda v: v.tensor_copy(rbuf, PS[3]))
    mb2_ev = inc("vector", sV)

    out_evs = {}
    out_store = {}
    for oc in range(ECH):
        wait("sync", sV, out_evs.get(oc - 2, 0))
        if oc == 0:
            wait("sync", sV, mb2_ev)
        hl = dma("sync", sL, stA[:, 1024 * (oc % 2):1024 * (oc % 2 + 1)],
                 h2p_scr[128 * oc:128 * (oc + 1), :])
        wait("vector", sL, hl)
        wait("vector", sL, out_store.get(oc - 2, 0))
        if oc == 0:
            wait("vector", sV, mb2_ev)
        hp = stA[:, 1024 * (oc % 2):1024 * (oc % 2 + 1)]
        t1 = stB[:, 1024 * (oc % 2):1024 * (oc % 2 + 1)]
        on("vector", lambda v, hp=hp, t1=t1: v.tensor_tensor(t1, hp, recipb, ALU.subtract))
        on("vector", lambda v, t1=t1: v.tensor_tensor(t1, t1, rbuf, ALU.mult))
        out_evs[oc] = inc("vector", sV)
        wait("sync", sV, out_evs[oc])
        out_store[oc] = dma("sync", sL, outT[128 * oc:128 * (oc + 1), :], t1)

    if DEBUG:
        dma("sync", sL, dbg["d_q"], q_scr)
        dma("sync", sL, dbg["d_kv"], kv_g[0:T, :])
        dma("sync", sL, dbg["d_hpre"], hpre_scr)
        dma("sync", sL, dbg["d_h1"], h1f_scr)
        dma("sync", sL, dbg["d_x3"], x3_scr[0:512, :])
        dma("sync", sL, dbg["d_gu"], gu_g[0:1024, :])
        dma("sync", sL, dbg["d_dd"], d_g[0:256, :])
        dma("sync", sL, dbg["d_h2"], h2p_scr)

    final = sL.n
    wait("sync", sL, final)
    return emit_all()


def stage_inputs(inputs):
    f32 = np.float32
    x = np.asarray(inputs["L0_attn_x"], f32)[0]
    xT = np.ascontiguousarray(x.T)

    def wstage(w, ndch):
        o, e = w.shape
        return np.ascontiguousarray(
            np.asarray(w, f32).reshape(ndch, 128, e // 128, 128).transpose(0, 3, 2, 1)
        ).reshape(ndch, 128, e)

    WQ = wstage(inputs["L0_attn_Wq"], 32)
    WK = wstage(inputs["L0_attn_Wk"], 8)
    WV = wstage(inputs["L0_attn_Wv"], 8)
    WO = wstage(inputs["L0_attn_Wo"], 32)
    bo_r = np.ascontiguousarray(np.asarray(inputs["L0_attn_bo"], f32).reshape(32, 128).T)

    gcT = np.ascontiguousarray(np.asarray(inputs["L0_ffn_g_codes"], np.int32).T)
    ucT = np.ascontiguousarray(np.asarray(inputs["L0_ffn_u_codes"], np.int32).T)
    dcT_full = np.zeros((DFF_PAD, E), np.int32)
    dcT_full[:DFF] = np.asarray(inputs["L0_ffn_d_codes"], np.int32).T
    gax = np.ascontiguousarray(np.asarray(inputs["L0_ffn_g_absmax"], f32).T)
    uax = np.ascontiguousarray(np.asarray(inputs["L0_ffn_u_absmax"], f32).T)
    dax_full = np.zeros((176, E), f32)
    dax_full[:172] = np.asarray(inputs["L0_ffn_d_absmax"], f32).T

    def astage(a, nch):
        return np.ascontiguousarray(
            np.asarray(a, f32).reshape(nch, 128, 16).transpose(1, 0, 2)).reshape(128, nch * 16)

    Ag = astage(inputs["L0_ffn_g_lora_a"], 32)
    Au = astage(inputs["L0_ffn_u_lora_a"], 32)
    Ad = astage(inputs["L0_ffn_d_lora_a"], FCH)
    Bg = np.ascontiguousarray(np.asarray(inputs["L0_ffn_g_lora_b"], f32))
    Bu = np.ascontiguousarray(np.asarray(inputs["L0_ffn_u_lora_b"], f32))
    Bd = np.ascontiguousarray(np.asarray(inputs["L0_ffn_d_lora_b"], f32))
    ident = np.eye(128, dtype=ml_dtypes.bfloat16)

    in_maps = []
    for r in range(NCORES):
        in_maps.append({
            "xTs": np.ascontiguousarray(xT[:, T * r:T * (r + 1)]),
            "WQ": WQ, "WK": WK, "WV": WV, "WO": WO, "bo_r": bo_r,
            "gcT": np.ascontiguousarray(gcT[GUSH * r:GUSH * (r + 1)]),
            "ucT": np.ascontiguousarray(ucT[GUSH * r:GUSH * (r + 1)]),
            "dcT": np.ascontiguousarray(dcT_full[DSH * r:DSH * (r + 1)]),
            "gax": np.ascontiguousarray(gax[8 * r:8 * (r + 1)]),
            "uax": np.ascontiguousarray(uax[8 * r:8 * (r + 1)]),
            "dax": np.ascontiguousarray(dax_full[22 * r:22 * (r + 1)]),
            "Ag": Ag, "Au": Au, "Ad": Ad, "Bg": Bg, "Bu": Bu, "Bd": Bd,
            "ident": ident,
        })
    return in_maps


LAST = [None]


def kernel(**inputs):
    in_maps = stage_inputs(inputs)
    nc = build_program()
    res = run_bass_kernel_spmd(nc, in_maps, core_ids=list(range(NCORES)))
    LAST[0] = res
    parts = [np.ascontiguousarray(r["outT"].T) for r in res.results]
    out = np.concatenate(parts, axis=0).reshape(1, NCORES * T, E).astype(np.float32)
    return out


# version 14
# speedup vs baseline: 1.0223x; 1.0223x over previous
import math
import numpy as np
import ml_dtypes
import concourse.bass as bass
import concourse.mybir as mybir
from concourse.bass_utils import run_bass_kernel_spmd

F32 = mybir.dt.float32
BF16 = mybir.dt.bfloat16
I32 = mybir.dt.int32
AF = mybir.ActivationFunctionType
ALU = mybir.AluOpType

NF4 = [-1.0, -0.6961928009986877, -0.5250730514526367, -0.39491748809814453,
       -0.28444138169288635, -0.18477343022823334, -0.09105003625154495, 0.0,
       0.07958029955625534, 0.16093020141124725, 0.2461123019456863,
       0.33791524171829224, 0.44070982933044434, 0.5626170039176941,
       0.7229568362236023, 1.0]

NCORES = 8
T = 1024
E = 4096
DFF = 11008
NH = 8
SCALE = 1.0 / math.sqrt(128.0)
EXP_BIAS = -20.0
DFF_PAD = 11264
DSH = 1408
GUSH = 512
ECH = E // 128
FCH = DFF // 128
DCH = DSH // 128
EPS = 1e-5

DEBUG = False
PHASE_STOP = "full"


class Sem:
    def __init__(self, cm):
        self._cm = cm
        self.h = cm.__enter__()
        self.n = 0

    def bump(self, k):
        self.n += k
        return self.n


def build_program():
    nc = bass.Bass("TRN2", target_bir_lowering=False, num_devices=NCORES)

    def inp(name, shape, dt=F32):
        return nc.dram_tensor(name, list(shape), dt, kind="ExternalInput").ap()

    xTs = inp("xTs", (E, T))
    WQ = inp("WQ", (ECH, 128, E))
    WK = inp("WK", (NH, 128, E))
    WV = inp("WV", (NH, 128, E))
    WO = inp("WO", (ECH, 128, E))
    bo_in = inp("bo_r", (128, ECH))
    gcT = inp("gcT", (GUSH, DFF), I32)
    ucT = inp("ucT", (GUSH, DFF), I32)
    dcT = inp("dcT", (DSH, E), I32)
    gax = inp("gax", (8, DFF))
    uax = inp("uax", (8, DFF))
    dax = inp("dax", (22, E))
    Ag = inp("Ag", (128, 512))
    Au = inp("Au", (128, 512))
    Ad = inp("Ad", (128, 16 * FCH))
    Bg = inp("Bg", (16, DFF))
    Bu = inp("Bu", (16, DFF))
    Bd = inp("Bd", (16, E))
    ident_in = inp("ident", (128, 128), BF16)

    outT = nc.dram_tensor("outT", [E, T], F32, kind="ExternalOutput").ap()
    dbg = {}
    if DEBUG:
        for nm, shape, dt in (("d_q", (E, T), BF16), ("d_kv", (T, 2048), BF16),
                              ("d_ot", (E, T), BF16),
                              ("d_hpre", (E, T), F32), ("d_h1", (E, T), F32),
                              ("d_x3", (512, T), BF16), ("d_gu", (1024, DFF), BF16),
                              ("d_dd", (256, E), BF16), ("d_h2", (E, T), F32)):
            dbg[nm] = nc.dram_tensor(nm, list(shape), dt, kind="ExternalOutput").ap()

    kv_con = nc.dram_tensor("kv_con", [T, 2048], BF16).ap()
    kv_g = nc.dram_tensor("kv_g", [NCORES * T, 2048], BF16, addr_space="Shared").ap()
    g_con = nc.dram_tensor("g_con", [GUSH, DFF], BF16).ap()
    g_g = nc.dram_tensor("g_g", [E, DFF], BF16, addr_space="Shared").ap()
    u_con = nc.dram_tensor("u_con", [GUSH, DFF], BF16).ap()
    u_g = nc.dram_tensor("u_g", [E, DFF], BF16, addr_space="Shared").ap()
    d_con = nc.dram_tensor("d_con", [DSH, E], BF16).ap()
    d_g = nc.dram_tensor("d_g", [DFF_PAD, E], BF16, addr_space="Shared").ap()
    q_scr = nc.dram_tensor("q_scr", [E, T], BF16).ap()
    hpre_scr = nc.dram_tensor("hpre_scr", [E, T], F32).ap()
    h1f_scr = nc.dram_tensor("h1f_scr", [E, T], F32).ap()
    h2p_scr = nc.dram_tensor("h2p_scr", [E, T], F32).ap()
    x3_scr = nc.dram_tensor("x3_scr", [DFF, T], BF16).ap()

    off = [16384]

    def sb(name, shape, dt, at=None, align=32):
        nbytes = int(np.prod(shape[1:])) * mybir.dt.size(dt)
        if at is None:
            at = (off[0] + align - 1) // align * align
            off[0] = at + nbytes
        t = nc.alloc_sbuf_tensor_at(name, list(shape), dt, offset=at)
        return t.ap()

    ones_b = sb("ones_b", [128, 1], BF16)
    cbias = sb("cbias", [128, 2], F32)
    ones_f = sb("ones_f", [128, 1], F32)
    onesr_f = sb("onesr_f", [128, 128], F32)
    ident = sb("ident_sb", [128, 128], BF16)
    bo_sb = sb("bo_sb", [128, ECH], F32)
    rowA = sb("rowA", [128, 1024], F32)
    rowB = sb("rowB", [128, 1024], F32)
    xabuf = sb("xabuf", [128, 1024], BF16)
    lorb = sb("lorb", [128, 1024], BF16)
    PT2 = sb("PT2", [128, 2048], BF16)
    x3f1 = sb("x3f1", [128, 2048], BF16)
    recipb = sb("recipb", [128, 1024], F32)
    rbuf = sb("rbuf", [128, 1024], F32)
    kvstage = sb("kvstage", [128, 2048], BF16)

    deq0 = off[0]
    codes8 = sb("codes8", [128, 3072], BF16)
    dacc = sb("dacc", [128, 1024], F32)
    dtmp = sb("dtmp", [128, 1024], F32)
    absb4 = sb("absb4", [128, 2048], F32)
    dout2 = sb("dout2", [128, 2048], BF16)
    h2t = sb("h2t", [128, 4096], F32, at=deq0)
    x3s2 = sb("x3s2", [128, 2048], BF16, at=deq0 + 16384)
    wdt = sb("wdt", [128, 1024], BF16, at=deq0 + 16384 + 4096)

    mid0 = off[0]
    stA = sb("stA", [128, 2048], F32)
    stB = sb("stB", [128, 2048], F32)
    VhB = sb("VhB", [128, 8192], BF16, at=mid0)
    stC = sb("stC", [128, 1024], F32)
    wblk2 = sb("wblk2", [128, 8192], BF16)
    r10 = off[0]
    xT_b = sb("xT_b", [128, 32768], BF16)
    OT_b = sb("OT_b", [128, 32768], BF16, at=r10)
    h1b = sb("h1b", [128, 32768], BF16, at=r10)
    r20 = r10 + 65536
    Kh2 = sb("Kh2", [128, 16384], BF16, at=r20)
    wgu = sb("wgu", [128, 16384], BF16, at=r20)
    Vh1 = sb("Vh1", [128, 8192], BF16, at=r20 + 32768)
    total_sbuf = r20 + 32768 + 16384
    assert total_sbuf <= 229376, f"sbuf over budget: {total_sbuf}"

    PS = [nc.alloc_psum_tensor(f"PS{i}", [128, 1024], F32).ap() for i in range(4)]
    PSbt = [p.bitcast(BF16) for p in PS]

    sL = Sem(nc.semaphore("sL"))
    sG = Sem(nc.semaphore("sG"))
    sC = Sem(nc.semaphore("sC"))
    sMM = Sem(nc.semaphore("sMM"))
    sV = Sem(nc.semaphore("sV"))
    sA = Sem(nc.semaphore("sA"))

    plan = {"tensor": [], "vector": [], "scalar": [], "gpsimd": [], "sync": []}

    def on(eng, fn):
        plan[eng].append(fn)

    def dma(q, sem, out, in_, **kw):
        tgt = sem.bump(16)
        on(q, lambda e: e.dma_start(out=out, in_=in_, **kw).then_inc(sem.h, 16))
        return tgt

    def wait(eng, sem, val):
        if val > 0:
            on(eng, lambda e: e.wait_ge(sem.h, val))

    def mm(o, l, r, st, sp):
        on("tensor", lambda e: e.matmul(o, l, r, start=st, stop=sp))

    def inc(eng, sem):
        tgt = sem.bump(1)
        if plan[eng]:
            prev = plan[eng][-1]
            plan[eng][-1] = (lambda e, p=prev: p(e).then_inc(sem.h, 1))
        else:
            on(eng, lambda e: e.nop().then_inc(sem.h, 1))
        return tgt

    deq_state = {"g": 0, "dv": {}, "st": {}}
    deq_groups = []
    for csrc, axs, cdst_t, n_ch in ((gcT, gax, g_con, 4), (ucT, uax, u_con, 4)):
        for ch in range(n_ch):
            for fi in range(11):
                f0 = fi * 1024
                w = min(1024, DFF - f0)
                deq_groups.append(("gu", csrc[128 * ch:128 * (ch + 1), f0:f0 + w],
                                   axs, 2 * ch, f0, w,
                                   cdst_t[128 * ch:128 * (ch + 1), f0:f0 + w]))
    for ch in range(DCH):
        for fi in range(4):
            f0 = fi * 1024
            deq_groups.append(("d", dcT[128 * ch:128 * (ch + 1), f0:f0 + 1024],
                               dax, 2 * ch, f0, 1024,
                               d_con[128 * ch:128 * (ch + 1), f0:f0 + 1024]))
    N_GU_GROUPS = 88

    deq_enabled = PHASE_STOP in ("pq", "pk", "p1", "f1", "full")
    if PHASE_STOP in ("pq", "pk", "p1"):
        deq_enabled = PHASE_STOP == "p1"

    def emit_deq(n):
        if not deq_enabled:
            return
        s = deq_state
        for _ in range(n):
            g = s["g"]
            if g >= len(deq_groups):
                return
            kind, csrc, axs, axr, f0, w, cdst = deq_groups[g]
            cb = codes8[:, 1024 * (g % 3):1024 * (g % 3) + w]
            ab = absb4[:, 1024 * (g % 2):1024 * (g % 2) + w]
            ob = dout2[:, 1024 * (g % 2):1024 * (g % 2) + w]
            wait("gpsimd", sV, s["dv"].get(g - 3, 0))
            cl = dma("gpsimd", sG, cb, csrc)
            wait("gpsimd", sV, s["dv"].get(g - 2, 0))
            a1 = dma("gpsimd", sG, ab[0:64, :], axs[axr:axr + 1, f0:f0 + w].partition_broadcast(64))
            a2 = dma("gpsimd", sG, ab[64:128, :], axs[axr + 1:axr + 2, f0:f0 + w].partition_broadcast(64))
            wait("vector", sG, cl)
            wait("vector", sL, s["st"].get(g - 2, 0))
            on("vector", lambda v, cb=cb: v.tensor_scalar(
                dacc[:, 0:cb.shape[1]], cb, 0.0, float(NF4[0]), ALU.is_equal, ALU.mult))
            for j in range(1, 16):
                on("vector", lambda v, cb=cb, j=j: v.tensor_scalar(
                    dtmp[:, 0:cb.shape[1]], cb, float(j), float(NF4[j]), ALU.is_equal, ALU.mult))
                on("vector", lambda v, w=w: v.tensor_tensor(
                    dacc[:, 0:w], dacc[:, 0:w], dtmp[:, 0:w], ALU.add))
            wait("vector", sG, a2)
            on("vector", lambda v, ob=ob, ab=ab, w=w: v.tensor_tensor(
                ob, dacc[:, 0:w], ab, ALU.mult))
            s["dv"][g] = inc("vector", sV)
            wait("sync", sV, s["dv"][g])
            s["st"][g] = dma("sync", sL, cdst, ob)
            s["g"] = g + 1
            for cnt, cin, cout in ((44, g_con, g_g), (88, u_con, u_g),
                                   (len(deq_groups), d_con, d_g)):
                if s["g"] == cnt:
                    wait("gpsimd", sL, s["st"][cnt - 1])
                    on("gpsimd", lambda gg, cin=cin, cout=cout: gg.collective_compute(
                        "AllGather", ALU.bypass, replica_groups=[list(range(NCORES))],
                        ins=[cin], outs=[cout]).then_inc(sC.h, 1))
                    sC.bump(1)

    on("vector", lambda v: v.memset(ones_b, 1.0))
    on("vector", lambda v: v.memset(ones_f, 1.0))
    on("vector", lambda v: v.memset(onesr_f, 1.0))
    on("vector", lambda v: v.memset(cbias[:, 0:1], EXP_BIAS))
    on("vector", lambda v: v.memset(cbias[:, 1:2], EPS))
    consts_v = inc("vector", sV)
    ident_ld = dma("sync", sL, ident, ident_in)
    bo_ld = dma("sync", sL, bo_sb, bo_in)

    for c in range(ECH):
        dma("gpsimd", sG, xT_b[:, 1024 * c:1024 * (c + 1)], xTs[128 * c:128 * (c + 1), :])
    xT_done = sG.n

    p1_mm_done = {}
    p1_sink_done = {}
    p1_gd = [0]

    def proj_stream(win, n_dch, sink, deq_every=0):
        mm_done = p1_mm_done
        sink_done = p1_sink_done
        for d in range(n_dch):
            gd = p1_gd[0]
            p1_gd[0] += 1
            wb = wblk2[:, 4096 * (gd % 2):4096 * (gd % 2 + 1)]
            wait("gpsimd", sMM, mm_done.get(gd - 2, 0))
            wld = dma("gpsimd", sG, wb, win[d])
            wait("tensor", sG, wld)
            if gd == 0:
                wait("tensor", sG, xT_done)
            ps = PS[gd % 2]
            tok = sink_done.get(gd - 2)
            if tok is not None:
                wait("tensor", tok[0], tok[1])
            for c in range(ECH):
                for half in range(2):
                    mm(ps[:, 512 * half:512 * (half + 1)],
                       wb[:, 128 * c:128 * (c + 1)],
                       xT_b[:, 1024 * c + 512 * half:1024 * c + 512 * (half + 1)],
                       c == 0, c == ECH - 1)
            mm_done[gd] = inc("tensor", sMM)
            wait("vector", sMM, mm_done[gd])
            tok = sink_done.get(gd - 2)
            if tok is not None:
                wait("vector", tok[0], tok[1])
            stg = kvstage[:, 1024 * (gd % 2):1024 * (gd % 2 + 1)]
            on("vector", lambda v, s=stg, p=ps: v.tensor_copy(s, p))
            ev = inc("vector", sV)
            sink_done[gd] = sink(d, stg, ev)
            if deq_every and d % deq_every == deq_every - 1:
                emit_deq(1)

    def q_sink(d, stg, ev):
        wait("sync", sV, ev)
        t = dma("sync", sL, q_scr[128 * d:128 * (d + 1), :], stg)
        return (sL, t)

    def k_sink(d, stg, ev):
        wait("sync", sV, ev)
        t = dma("sync", sL, kv_con[128 * d:128 * (d + 1), 0:1024], stg)
        return (sL, t)

    tr_prev = {}

    tr_store = {}

    def v_sink(d, stg, ev):
        wait("tensor", sV, ev)
        if d == 0:
            wait("tensor", sL, ident_ld)
        last = None
        for sub in range(8):
            gsub = 8 * d + sub
            pb = PSbt[2][:, 1024 * (sub % 2):1024 * (sub % 2) + 128]
            tok = tr_prev.get(gsub - 2)
            if tok:
                wait("tensor", sV, tok)
            on("tensor", lambda e, o=pb, i=stg[:, 128 * sub:128 * (sub + 1)]:
               e.transpose(o, i, ident))
            td = inc("tensor", sMM)
            wait("vector", sMM, td)
            wait("vector", sL, tr_store.get(gsub - 2, 0))
            tout = x3f1[:, 128 * (sub % 2):128 * (sub % 2) + 128]
            on("vector", lambda v, o=tout, p=pb: v.tensor_copy(o, p))
            ev2 = inc("vector", sV)
            tr_prev[gsub] = ev2
            wait("sync", sV, ev2)
            last = dma("sync", sL,
                       kv_con[128 * sub:128 * (sub + 1), 1024 + 128 * d:1024 + 128 * (d + 1)],
                       tout)
            tr_store[gsub] = last
        return (sL, last)

    if PHASE_STOP != "p0":
        if PHASE_STOP != "pk":
            proj_stream([WQ[d] for d in range(ECH)], ECH, q_sink, deq_every=2)
        if PHASE_STOP != "pq":
            proj_stream([WK[d] for d in range(NH)], NH, k_sink, deq_every=2)
        if PHASE_STOP not in ("pq", "pk"):
            proj_stream([WV[d] for d in range(NH)], NH, v_sink, deq_every=2)
    kv_stores_done = sL.n
    phase1_mm = sMM.n

    def dummy_out_and_emit():
        on("vector", lambda v: v.memset(stB[:, 0:1024], 0.0))
        zv = inc("vector", sV)
        wait("sync", sV, zv)
        for oc in range(ECH):
            dma("sync", sL, outT[128 * oc:128 * (oc + 1), :], stB[:, 0:1024])
        wait("sync", sL, sL.n)
        return emit_all()

    def emit_all():
        with nc.Block() as block:
            @block.tensor
            def _(e):
                for fn in plan["tensor"]:
                    fn(e)

            @block.vector
            def _(e):
                for fn in plan["vector"]:
                    fn(e)

            @block.scalar
            def _(e):
                for fn in plan["scalar"]:
                    fn(e)

            @block.gpsimd
            def _(e):
                for fn in plan["gpsimd"]:
                    fn(e)

            @block.sync
            def _(e):
                for fn in plan["sync"]:
                    fn(e)
        return nc

    if PHASE_STOP in ("p0", "pq", "pk", "p1"):
        return dummy_out_and_emit()

    wait("gpsimd", sL, kv_stores_done)
    on("gpsimd", lambda g: g.collective_compute(
        "AllGather", ALU.bypass, replica_groups=[list(range(NCORES))],
        ins=[kv_con], outs=[kv_g]).then_inc(sC.h, 1))
    kv_gathered = sC.bump(1)

    wait("tensor", sV, consts_v)
    NU = 4096
    sc_done = {}
    ep_done = {}
    avs_done = {}
    rec_ev = {}
    ot_ev = {}
    kvload = {}
    qtload = {}

    def khb(hd):
        return Kh2[:, 8192 * (hd % 2):8192 * (hd % 2) + 8192]

    def vhb(hd):
        return Vh1 if hd % 2 == 0 else VhB

    vload = {}

    def load_head(hd):
        kvload[hd] = dma(
            "sync", sL, khb(hd).rearrange("p (r c) -> p r c", r=NCORES),
            kv_g.rearrange("(r q) c -> q r c", q=1024)[128 * hd:128 * (hd + 1), :, 0:1024])

    def load_vh(hd):
        vload[hd] = dma(
            "sync", sL, vhb(hd).rearrange("p (k f) -> p k f", f=128),
            kv_g.rearrange("(k p) c -> p k c", p=128)[:, :, 1024 + 128 * hd:1024 + 128 * (hd + 1)])

    def load_qt(qh):
        qtload[qh] = dma("sync", sL, kvstage[:, 1024 * (qh % 2):1024 * (qh % 2 + 1)],
                         q_scr[128 * qh:128 * (qh + 1), :])

    def score_bank(u):
        j = u % 4
        return PS[j // 2][:, 512 * (j % 2):512 * (j % 2) + 512]

    def pt_slice(u):
        return PT2[:, 512 * (u % 4):512 * (u % 4) + 512]

    wait("sync", sC, kv_gathered)
    wait("sync", sMM, phase1_mm)
    load_head(0)
    load_head(1)
    load_vh(0)
    load_vh(1)
    load_qt(0)
    load_qt(1)

    def emit_avs(u):
        p = u // 64
        kc = u % 64
        hd = p // 8
        wait("tensor", sA, ep_done[u // 2])
        if kc == 0 and p > 0:
            wait("tensor", sV, ot_ev[p - 1])
        if kc == 0 and p % 8 == 0:
            wait("tensor", sL, vload[hd])
        mm(PS[2][:, 0:512], vhb(hd)[:, 128 * kc:128 * (kc + 1)], pt_slice(u),
           kc == 0, kc == 63)
        mm(PS[3][0:1, 0:512], ones_b, pt_slice(u), kc == 0, kc == 63)
        avs_done[u] = inc("tensor", sMM)

    def emit_recip(p):
        wait("vector", sMM, avs_done[64 * p + 63])
        on("vector", lambda v: v.reciprocal(rowA[0:1, 0:512], PS[3][0:1, 0:512]))
        rec_ev[p] = inc("vector", sV)

    def emit_epilogue(p):
        qh = p // 2
        half = p % 2
        wait("tensor", sV, rec_ev[p])
        mm(PS[2][:, 512:1024], onesr_f[0:1, :], rowA[0:1, 0:512], True, True)
        bc = inc("tensor", sMM)
        wait("vector", sMM, bc)
        on("vector", lambda v: v.tensor_copy(recipb[:, 0:512], PS[2][:, 512:1024]))
        on("vector", lambda v, qh=qh, half=half: v.tensor_tensor(
            OT_b[:, 1024 * qh + 512 * half:1024 * qh + 512 * half + 512],
            PS[2][:, 0:512], recipb[:, 0:512], ALU.mult))
        ot_ev[p] = inc("vector", sV)
        emit_deq(2 if deq_state["g"] < N_GU_GROUPS else 1)

    for u in range(NU):
        p = u // 64
        qh = p // 2
        half = p % 2
        kc = u % 64
        hd = qh // 4
        if u % 512 == 0:
            wait("tensor", sL, kvload[hd])
        if u % 128 == 0:
            wait("tensor", sL, qtload[qh])
        if u >= 4:
            wait("tensor", sA, ep_done[(u - 4) // 2])
        mm(score_bank(u), khb(hd)[:, 128 * kc:128 * (kc + 1)],
           kvstage[:, 1024 * (qh % 2) + 512 * half:1024 * (qh % 2) + 512 * half + 512],
           True, True)
        sc_done[u] = inc("tensor", sMM)
        if u % 512 == 3 and hd >= 1 and hd + 1 < NH:
            wait("sync", sMM, sc_done[512 * hd - 1])
            load_head(hd + 1)
        if u % 128 == 3 and qh >= 1 and qh + 1 < 32:
            wait("sync", sMM, sc_done[128 * qh - 1])
            load_qt(qh + 1)
        if u % 2 == 1:
            q2 = u // 2
            wait("scalar", sMM, sc_done[u])
            if q2 >= 2:
                wait("scalar", sMM, avs_done.get(2 * q2 - 3, 0))
            pj = (u - 1) % 4
            psrc = PS[pj // 2]
            pdst = PT2[:, 512 * pj:512 * pj + 1024]
            on("scalar", lambda s, o=pdst, i=psrc: s.activation(
                o, i, AF.Exp, bias=cbias[:, 0:1], scale=SCALE))
            ep_done[q2] = inc("scalar", sA)
        if u >= 3:
            u2 = u - 3
            emit_avs(u2)
            if u2 % 64 == 63:
                emit_recip(u2 // 64)
                emit_epilogue(u2 // 64)
            if u2 % 512 == 2 and u2 >= 512 and u2 // 512 + 1 < NH:
                wait("sync", sMM, avs_done[512 * (u2 // 512) - 1])
                load_vh(u2 // 512 + 1)
    for u2 in range(NU - 3, NU):
        emit_avs(u2)
        if u2 % 64 == 63:
            emit_recip(u2 // 64)
            emit_epilogue(u2 // 64)
    flash_last_mm = sMM.n
    flash_done_v = ot_ev[63]
    if PHASE_STOP == "flash":
        return dummy_out_and_emit()

    if DEBUG:
        wait("sync", sV, flash_done_v)
        for oc in range(ECH):
            dma("sync", sL, dbg["d_ot"][128 * oc:128 * (oc + 1), :],
                OT_b[:, 1024 * oc:1024 * (oc + 1)])
    wait("gpsimd", sV, flash_done_v)
    copy_done = {}
    hp_evs = {}
    hp_store = {}
    ln1_mm = {}
    for oc in range(ECH):
        wait("gpsimd", sMM, ln1_mm.get(oc - 2, 0))
        wld = dma("gpsimd", sG, wblk2[:, 4096 * (oc % 2):4096 * (oc % 2 + 1)], WO[oc])
        wait("sync", sV, hp_evs.get(oc - 2, 0))
        if oc == 0:
            wait("sync", sMM, flash_last_mm)
        xld = dma("sync", sL, stB[:, 1024 * (oc % 2):1024 * (oc % 2 + 1)],
                  xTs[128 * oc:128 * (oc + 1), :])
        wait("tensor", sG, wld)
        if oc == 0:
            wait("tensor", sV, flash_done_v)
            wait("tensor", sL, bo_ld)
        ps = PS[oc % 2]
        wait("tensor", sV, hp_evs.get(oc - 2, 0))
        for c in range(ECH):
            for half in range(2):
                mm(ps[:, 512 * half:512 * (half + 1)],
                   wblk2[:, 4096 * (oc % 2) + 128 * c:4096 * (oc % 2) + 128 * (c + 1)],
                   OT_b[:, 1024 * c + 512 * half:1024 * c + 512 * (half + 1)],
                   c == 0, c == ECH - 1)
        mmd = inc("tensor", sMM)
        wait("vector", sMM, mmd)
        wait("vector", sL, xld)
        wait("vector", sL, hp_store.get(oc - 2, 0))
        wait("vector", sMM, ln1_mm.get(oc - 2, 0))
        hp = stA[:, 1024 * (oc % 2):1024 * (oc % 2 + 1)]
        on("vector", lambda v, hp=hp, p=ps, oc=oc: v.scalar_tensor_tensor(
            hp, p, bo_sb[:, oc:oc + 1], stB[:, 1024 * (oc % 2):1024 * (oc % 2 + 1)],
            ALU.add, ALU.add))
        hp_evs[oc] = inc("vector", sV)
        wait("sync", sV, hp_evs[oc])
        hp_store[oc] = dma("sync", sL, hpre_scr[128 * oc:128 * (oc + 1), :], hp)
        wait("scalar", sV, hp_evs[oc])
        wait("scalar", sMM, ln1_mm.get(oc - 1, 0))
        on("scalar", lambda s, hp=hp: s.activation(stC, hp, AF.Square, bias=0.0, scale=1.0))
        sq_ev = inc("scalar", sA)
        wait("tensor", sA, sq_ev)
        for half in range(2):
            mm(PS[2][0:1, 512 * half:512 * (half + 1)], ones_f,
               hp[:, 512 * half:512 * (half + 1)], oc == 0, oc == ECH - 1)
        for half in range(2):
            mm(PS[3][0:1, 512 * half:512 * (half + 1)], ones_f,
               stC[:, 512 * half:512 * (half + 1)], oc == 0, oc == ECH - 1)
        ln1_mm[oc] = inc("tensor", sMM)
        emit_deq(1)
    ln1_mm_final = ln1_mm[ECH - 1]

    def ln_rows(psum_z_row, psum_z2_row, gate_mm):
        wait("vector", sMM, gate_mm)
        on("vector", lambda v: v.tensor_scalar(rowA[32:33, :], psum_z_row,
                                               1.0 / E, None, ALU.mult))
        on("vector", lambda v: v.tensor_scalar(rowB[0:1, :], psum_z2_row,
                                               1.0 / E, None, ALU.mult))
        on("vector", lambda v: v.tensor_tensor(rowA[0:1, :], rowA[32:33, :],
                                               rowA[32:33, :], ALU.mult))
        on("vector", lambda v: v.tensor_tensor(rowB[0:1, :], rowB[0:1, :],
                                               rowA[0:1, :], ALU.subtract))
        var_ev = inc("vector", sV)
        wait("scalar", sV, var_ev)
        on("scalar", lambda s: s.activation(rowB[0:1, :], rowB[0:1, :],
                                            AF.Sqrt, bias=cbias[0:1, 1:2], scale=1.0))
        sq2 = inc("scalar", sA)
        wait("vector", sA, sq2)
        on("vector", lambda v: v.reciprocal(rowA[64:65, :], rowB[0:1, :]))
        rr_ev = inc("vector", sV)
        wait("tensor", sV, rr_ev)
        for half in range(2):
            mm(PS[0][:, 512 * half:512 * (half + 1)], onesr_f[32:33, :],
               rowA[32:33, 512 * half:512 * (half + 1)], True, True)
            mm(PS[1][:, 512 * half:512 * (half + 1)], onesr_f[64:65, :],
               rowA[64:65, 512 * half:512 * (half + 1)], True, True)
        bc2 = inc("tensor", sMM)
        wait("vector", sMM, bc2)
        on("vector", lambda v: v.tensor_copy(recipb, PS[0]))
        on("vector", lambda v: v.tensor_copy(rbuf, PS[1]))
        return inc("vector", sV)

    mb_ev = ln_rows(PS[2][0:1, :], PS[3][0:1, :], ln1_mm_final)

    h1_store = {}
    h1_evs = {}
    for oc in range(ECH):
        wait("sync", sV, h1_evs.get(oc - 2, 0))
        if oc == 0:
            wait("sync", sMM, ln1_mm_final)
        hl = dma("sync", sL, stA[:, 1024 * (oc % 2):1024 * (oc % 2 + 1)],
                 hpre_scr[128 * oc:128 * (oc + 1), :])
        wait("vector", sL, hl)
        wait("vector", sL, h1_store.get(oc - 2, 0))
        if oc == 0:
            wait("vector", sV, mb_ev)
        hp = stA[:, 1024 * (oc % 2):1024 * (oc % 2 + 1)]
        t1 = stB[:, 1024 * (oc % 2):1024 * (oc % 2 + 1)]
        on("vector", lambda v, hp=hp, t1=t1: v.tensor_tensor(t1, hp, recipb, ALU.subtract))
        on("vector", lambda v, t1=t1: v.tensor_tensor(t1, t1, rbuf, ALU.mult))
        on("vector", lambda v, t1=t1, oc=oc: v.tensor_copy(
            h1b[:, 1024 * oc:1024 * (oc + 1)], t1))
        h1_evs[oc] = inc("vector", sV)
        wait("sync", sV, h1_evs[oc])
        h1_store[oc] = dma("sync", sL, h1f_scr[128 * oc:128 * (oc + 1), :], t1)
        if oc % 2 == 1:
            emit_deq(1)
    h1_ready = h1_evs[ECH - 1]
    if PHASE_STOP == "ln1":
        return dummy_out_and_emit()

    wait("gpsimd", sMM, ln1_mm_final)
    agl = dma("gpsimd", sG, wgu[:, 0:512], Ag)
    aul = dma("gpsimd", sG, wgu[:, 512:1024], Au)
    wait("tensor", sG, aul)
    wait("tensor", sV, h1_ready)
    for c in range(ECH):
        for half in range(2):
            mm(PS[2][0:16, 512 * half:512 * (half + 1)],
               wgu[:, 16 * c:16 * (c + 1)],
               h1b[:, 1024 * c + 512 * half:1024 * c + 512 * (half + 1)],
               c == 0, c == ECH - 1)
        for half in range(2):
            mm(PS[3][0:16, 512 * half:512 * (half + 1)],
               wgu[:, 512 + 16 * c:512 + 16 * (c + 1)],
               h1b[:, 1024 * c + 512 * half:1024 * c + 512 * (half + 1)],
               c == 0, c == ECH - 1)
    xa_mm = inc("tensor", sMM)
    wait("vector", sMM, xa_mm)
    on("vector", lambda v: v.tensor_copy(xabuf[0:16, :], PS[2][0:16, :]))
    on("vector", lambda v: v.tensor_copy(xabuf[32:48, :], PS[3][0:16, :]))
    xa_ev = inc("vector", sV)
    emit_deq(len(deq_groups))
    gu_gathered = 3
    d_gathered = 4
    assert sC.n == 4

    f1mm = {}
    f1_x3 = {}
    x3_store = {}
    relu_done = {}
    for fc in range(FCH):
        bb = 8192 * (fc % 2)
        wait("sync", sMM, f1mm.get(fc - 2, 0))
        if fc == 0:
            wait("sync", sC, gu_gathered)
            wait("sync", sMM, xa_mm)
        wl = dma("sync", sL,
                 wgu[:, bb:bb + 4096].rearrange("p (x f) -> p x f", f=128),
                 g_g.rearrange("(x p) f -> p x f", p=128)[:, :, 128 * fc:128 * (fc + 1)])
        wl = dma("sync", sL,
                 wgu[:, bb + 4096:bb + 8192].rearrange("p (x f) -> p x f", f=128),
                 u_g.rearrange("(x p) f -> p x f", p=128)[:, :, 128 * fc:128 * (fc + 1)])
        wait("gpsimd", sMM, f1mm.get(fc - 2, 0))
        bgl = dma("gpsimd", sG, lorb[0:16, 128 * (fc % 2):128 * (fc % 2) + 128],
                  Bg[:, 128 * fc:128 * (fc + 1)])
        bul = dma("gpsimd", sG, lorb[32:48, 128 * (fc % 2):128 * (fc % 2) + 128],
                  Bu[:, 128 * fc:128 * (fc + 1)])
        wait("tensor", sL, wl)
        wait("tensor", sG, bul)
        if fc == 0:
            wait("tensor", sV, xa_ev)
        psg = PS[0] if fc % 2 == 0 else PS[2]
        psu = PS[1] if fc % 2 == 0 else PS[3]
        wait("tensor", sV, f1_x3.get(fc - 2, 0))
        for c in range(ECH):
            for half in range(2):
                mm(psg[:, 512 * half:512 * (half + 1)],
                   wgu[:, bb + 128 * c:bb + 128 * (c + 1)],
                   h1b[:, 1024 * c + 512 * half:1024 * c + 512 * (half + 1)],
                   c == 0, False)
            for half in range(2):
                mm(psu[:, 512 * half:512 * (half + 1)],
                   wgu[:, bb + 4096 + 128 * c:bb + 4096 + 128 * (c + 1)],
                   h1b[:, 1024 * c + 512 * half:1024 * c + 512 * (half + 1)],
                   c == 0, False)
        for half in range(2):
            mm(psg[:, 512 * half:512 * (half + 1)],
               lorb[0:16, 128 * (fc % 2):128 * (fc % 2) + 128],
               xabuf[0:16, 512 * half:512 * (half + 1)], False, True)
            mm(psu[:, 512 * half:512 * (half + 1)],
               lorb[32:48, 128 * (fc % 2):128 * (fc % 2) + 128],
               xabuf[32:48, 512 * half:512 * (half + 1)], False, True)
        f1mm[fc] = inc("tensor", sMM)
        wait("scalar", sMM, f1mm[fc])
        wait("scalar", sV, f1_x3.get(fc - 2, 0))
        rt = PT2[:, 1024 * (fc % 2):1024 * (fc % 2 + 1)]
        on("scalar", lambda s, o=rt, p=psg: s.activation(o, p, AF.Relu, bias=0.0, scale=1.0))
        relu_done[fc] = inc("scalar", sA)
        wait("vector", sA, relu_done[fc])
        wait("vector", sL, x3_store.get(fc - 2, 0))
        x3t = x3f1[:, 1024 * (fc % 2):1024 * (fc % 2 + 1)]
        on("vector", lambda v, o=x3t, a=rt, p=psu: v.tensor_tensor(o, a, p, ALU.mult))
        f1_x3[fc] = inc("vector", sV)
        wait("sync", sV, f1_x3[fc])
        x3_store[fc] = dma("sync", sL, x3_scr[128 * fc:128 * (fc + 1), :], x3t)

    if PHASE_STOP == "f1":
        wait("sync", sL, x3_store[FCH - 1])
        return dummy_out_and_emit()

    wait("gpsimd", sMM, f1mm[FCH - 1])
    adl = dma("gpsimd", sG, wblk2[:, 0:16 * FCH], Ad)
    xadp = {}
    for fc in range(FCH):
        wait("sync", sMM, xadp.get(fc - 2, 0))
        xl = dma("sync", sL, x3f1[:, 1024 * (fc % 2):1024 * (fc % 2 + 1)],
                 x3_scr[128 * fc:128 * (fc + 1), :])
        wait("tensor", sL, xl)
        if fc == 0:
            wait("tensor", sG, adl)
            wait("tensor", sV, f1_x3[FCH - 1])
        for half in range(2):
            mm(PS[0][0:16, 512 * half:512 * (half + 1)],
               wblk2[:, 16 * fc:16 * (fc + 1)],
               x3f1[:, 1024 * (fc % 2) + 512 * half:1024 * (fc % 2) + 512 * (half + 1)],
               fc == 0, fc == FCH - 1)
        xadp[fc] = inc("tensor", sMM)
    wait("vector", sMM, xadp[FCH - 1])
    on("vector", lambda v: v.tensor_copy(xabuf[64:80, :], PS[0][0:16, :]))
    xad_ev = inc("vector", sV)

    stats_ev = 0
    f2p = {}
    h2_store = {}
    sz_init = False
    f2mm_prev = 0
    for s in range(8):
        wait("gpsimd", sMM, f2mm_prev)
        bdl = dma("gpsimd", sG, lorb[64:80, 512:1024], Bd[:, 512 * s:512 * (s + 1)])
        for fc in range(FCH):
            wait("sync", sMM, f2p.get((s, fc - 2), 0) if fc >= 2 else
                 f2p.get((s - 1, FCH - 1), 0))
            if s == 0 and fc == 0:
                wait("sync", sC, d_gathered)
            xl = dma("sync", sL, x3s2[:, 1024 * (fc % 2):1024 * (fc % 2 + 1)],
                     x3_scr[128 * fc:128 * (fc + 1), :])
            r, lr = fc // 11, fc % 11
            wl = dma("sync", sL, wdt[:, 512 * (fc % 2):512 * (fc % 2 + 1)],
                     d_g[DSH * r + 128 * lr:DSH * r + 128 * lr + 128, 512 * s:512 * (s + 1)])
            wait("tensor", sL, wl)
            if fc == 0:
                wait("tensor", sG, bdl)
                if s == 0:
                    wait("tensor", sV, xad_ev)
                else:
                    wait("tensor", sV, stats_ev)
            x3r = x3s2[:, 1024 * (fc % 2):1024 * (fc % 2 + 1)]
            for k in range(4):
                for half in range(2):
                    mm(PS[k][:, 512 * half:512 * (half + 1)],
                       wdt[:, 512 * (fc % 2) + 128 * k:512 * (fc % 2) + 128 * (k + 1)],
                       x3r[:, 512 * half:512 * (half + 1)], fc == 0, False)
            f2p[(s, fc)] = inc("tensor", sMM)
        for k in range(4):
            for half in range(2):
                mm(PS[k][:, 512 * half:512 * (half + 1)],
                   lorb[64:80, 512 + 128 * k:512 + 128 * (k + 1)],
                   xabuf[64:80, 512 * half:512 * (half + 1)], False, True)
        f2mm = inc("tensor", sMM)
        f2mm_prev = f2mm
        wait("vector", sMM, f2mm)
        sq_evs = {}
        for k in range(4):
            oc = 4 * s + k
            hl = dma("sync", sL, stB[:, 1024 * (k % 2):1024 * (k % 2 + 1)],
                     h1f_scr[128 * oc:128 * (oc + 1), :])
            wait("vector", sL, hl)
            wait("vector", sL, h2_store.get((s - 1, k), 0))
            on("vector", lambda v, k=k: v.tensor_tensor(
                h2t[:, 1024 * k:1024 * (k + 1)], PS[k],
                stB[:, 1024 * (k % 2):1024 * (k % 2 + 1)], ALU.add))
            ev = inc("vector", sV)
            wait("sync", sV, ev)
            h2_store[(s, k)] = dma("sync", sL, h2p_scr[128 * oc:128 * (oc + 1), :],
                                   h2t[:, 1024 * k:1024 * (k + 1)])
            sq_evs[k] = ev
        wait("tensor", sV, sq_evs[3])
        for k in range(4):
            wait("scalar", sV, sq_evs[k])
            on("scalar", lambda sc, k=k: sc.activation(
                stC, h2t[:, 1024 * k:1024 * (k + 1)], AF.Square, bias=0.0, scale=1.0))
            sqe = inc("scalar", sA)
            wait("tensor", sA, sqe)
            wait("tensor", sV, sq_evs[k])
            for half in range(2):
                mm(PS[0][0:1, 512 * half:512 * (half + 1)], ones_f,
                   h2t[:, 1024 * k + 512 * half:1024 * k + 512 * (half + 1)],
                   k == 0, k == 3)
            for half in range(2):
                mm(PS[1][0:1, 512 * half:512 * (half + 1)], ones_f,
                   stC[:, 512 * half:512 * (half + 1)], k == 0, k == 3)
            stk = inc("tensor", sMM)
            wait("scalar", sMM, stk)
        st_mm = stk
        wait("vector", sMM, st_mm)
        if not sz_init:
            on("vector", lambda v: v.tensor_copy(rowB[32:33, :], PS[0][0:1, :]))
            on("vector", lambda v: v.tensor_copy(rowB[64:65, :], PS[1][0:1, :]))
            sz_init = True
        else:
            on("vector", lambda v: v.tensor_tensor(rowB[32:33, :], rowB[32:33, :],
                                                   PS[0][0:1, :], ALU.add))
            on("vector", lambda v: v.tensor_tensor(rowB[64:65, :], rowB[64:65, :],
                                                   PS[1][0:1, :], ALU.add))
        stats_ev = inc("vector", sV)

    wait("vector", sV, stats_ev)
    on("vector", lambda v: v.tensor_scalar(rowA[32:33, :], rowB[32:33, :],
                                           1.0 / E, None, ALU.mult))
    on("vector", lambda v: v.tensor_scalar(rowB[0:1, :], rowB[64:65, :],
                                           1.0 / E, None, ALU.mult))
    on("vector", lambda v: v.tensor_tensor(rowA[0:1, :], rowA[32:33, :],
                                           rowA[32:33, :], ALU.mult))
    on("vector", lambda v: v.tensor_tensor(rowB[0:1, :], rowB[0:1, :],
                                           rowA[0:1, :], ALU.subtract))
    var2 = inc("vector", sV)
    wait("scalar", sV, var2)
    on("scalar", lambda s: s.activation(rowB[0:1, :], rowB[0:1, :],
                                        AF.Sqrt, bias=cbias[0:1, 1:2], scale=1.0))
    sq3 = inc("scalar", sA)
    wait("vector", sA, sq3)
    on("vector", lambda v: v.reciprocal(rowA[64:65, :], rowB[0:1, :]))
    rr2 = inc("vector", sV)
    wait("tensor", sV, rr2)
    wait("tensor", sV, stats_ev)
    for half in range(2):
        mm(PS[2][:, 512 * half:512 * (half + 1)], onesr_f[32:33, :],
           rowA[32:33, 512 * half:512 * (half + 1)], True, True)
        mm(PS[3][:, 512 * half:512 * (half + 1)], onesr_f[64:65, :],
           rowA[64:65, 512 * half:512 * (half + 1)], True, True)
    bc3 = inc("tensor", sMM)
    wait("vector", sMM, bc3)
    on("vector", lambda v: v.tensor_copy(recipb, PS[2]))
    on("vector", lambda v: v.tensor_copy(rbuf, PS[3]))
    mb2_ev = inc("vector", sV)

    out_evs = {}
    out_store = {}
    for oc in range(ECH):
        wait("sync", sV, out_evs.get(oc - 2, 0))
        if oc == 0:
            wait("sync", sV, mb2_ev)
        hl = dma("sync", sL, stA[:, 1024 * (oc % 2):1024 * (oc % 2 + 1)],
                 h2p_scr[128 * oc:128 * (oc + 1), :])
        wait("vector", sL, hl)
        wait("vector", sL, out_store.get(oc - 2, 0))
        if oc == 0:
            wait("vector", sV, mb2_ev)
        hp = stA[:, 1024 * (oc % 2):1024 * (oc % 2 + 1)]
        t1 = stB[:, 1024 * (oc % 2):1024 * (oc % 2 + 1)]
        on("vector", lambda v, hp=hp, t1=t1: v.tensor_tensor(t1, hp, recipb, ALU.subtract))
        on("vector", lambda v, t1=t1: v.tensor_tensor(t1, t1, rbuf, ALU.mult))
        out_evs[oc] = inc("vector", sV)
        wait("sync", sV, out_evs[oc])
        out_store[oc] = dma("sync", sL, outT[128 * oc:128 * (oc + 1), :], t1)

    if DEBUG:
        dma("sync", sL, dbg["d_q"], q_scr)
        dma("sync", sL, dbg["d_kv"], kv_g[0:T, :])
        dma("sync", sL, dbg["d_hpre"], hpre_scr)
        dma("sync", sL, dbg["d_h1"], h1f_scr)
        dma("sync", sL, dbg["d_x3"], x3_scr[0:512, :])
        dma("sync", sL, dbg["d_gu"], gu_g[0:1024, :])
        dma("sync", sL, dbg["d_dd"], d_g[0:256, :])
        dma("sync", sL, dbg["d_h2"], h2p_scr)

    final = sL.n
    wait("sync", sL, final)
    return emit_all()


def stage_inputs(inputs):
    f32 = np.float32
    x = np.asarray(inputs["L0_attn_x"], f32)[0]
    xT = np.ascontiguousarray(x.T)

    def wstage(w, ndch):
        o, e = w.shape
        return np.ascontiguousarray(
            np.asarray(w, f32).reshape(ndch, 128, e // 128, 128).transpose(0, 3, 2, 1)
        ).reshape(ndch, 128, e)

    WQ = wstage(inputs["L0_attn_Wq"], 32)
    WK = wstage(inputs["L0_attn_Wk"], 8)
    WV = wstage(inputs["L0_attn_Wv"], 8)
    WO = wstage(inputs["L0_attn_Wo"], 32)
    bo_r = np.ascontiguousarray(np.asarray(inputs["L0_attn_bo"], f32).reshape(32, 128).T)

    gcT = np.ascontiguousarray(np.asarray(inputs["L0_ffn_g_codes"], np.int32).T)
    ucT = np.ascontiguousarray(np.asarray(inputs["L0_ffn_u_codes"], np.int32).T)
    dcT_full = np.zeros((DFF_PAD, E), np.int32)
    dcT_full[:DFF] = np.asarray(inputs["L0_ffn_d_codes"], np.int32).T
    gax = np.ascontiguousarray(np.asarray(inputs["L0_ffn_g_absmax"], f32).T)
    uax = np.ascontiguousarray(np.asarray(inputs["L0_ffn_u_absmax"], f32).T)
    dax_full = np.zeros((176, E), f32)
    dax_full[:172] = np.asarray(inputs["L0_ffn_d_absmax"], f32).T

    def astage(a, nch):
        return np.ascontiguousarray(
            np.asarray(a, f32).reshape(nch, 128, 16).transpose(1, 0, 2)).reshape(128, nch * 16)

    Ag = astage(inputs["L0_ffn_g_lora_a"], 32)
    Au = astage(inputs["L0_ffn_u_lora_a"], 32)
    Ad = astage(inputs["L0_ffn_d_lora_a"], FCH)
    Bg = np.ascontiguousarray(np.asarray(inputs["L0_ffn_g_lora_b"], f32))
    Bu = np.ascontiguousarray(np.asarray(inputs["L0_ffn_u_lora_b"], f32))
    Bd = np.ascontiguousarray(np.asarray(inputs["L0_ffn_d_lora_b"], f32))
    ident = np.eye(128, dtype=ml_dtypes.bfloat16)

    in_maps = []
    for r in range(NCORES):
        in_maps.append({
            "xTs": np.ascontiguousarray(xT[:, T * r:T * (r + 1)]),
            "WQ": WQ, "WK": WK, "WV": WV, "WO": WO, "bo_r": bo_r,
            "gcT": np.ascontiguousarray(gcT[GUSH * r:GUSH * (r + 1)]),
            "ucT": np.ascontiguousarray(ucT[GUSH * r:GUSH * (r + 1)]),
            "dcT": np.ascontiguousarray(dcT_full[DSH * r:DSH * (r + 1)]),
            "gax": np.ascontiguousarray(gax[8 * r:8 * (r + 1)]),
            "uax": np.ascontiguousarray(uax[8 * r:8 * (r + 1)]),
            "dax": np.ascontiguousarray(dax_full[22 * r:22 * (r + 1)]),
            "Ag": Ag, "Au": Au, "Ad": Ad, "Bg": Bg, "Bu": Bu, "Bd": Bd,
            "ident": ident,
        })
    return in_maps


LAST = [None]


def kernel(**inputs):
    in_maps = stage_inputs(inputs)
    nc = build_program()
    res = run_bass_kernel_spmd(nc, in_maps, core_ids=list(range(NCORES)))
    LAST[0] = res
    parts = [np.ascontiguousarray(r["outT"].T) for r in res.results]
    out = np.concatenate(parts, axis=0).reshape(1, NCORES * T, E).astype(np.float32)
    return out
